# version 02
# baseline (speedup 1.0000x reference)
import numpy as np
import ml_dtypes
import concourse.bass as bass
import concourse.mybir as mybir
from concourse.bass_utils import run_bass_kernel_spmd

F32 = mybir.dt.float32
BF16 = mybir.dt.bfloat16
AF = mybir.ActivationFunctionType
ALU = mybir.AluOpType
AX = mybir.AxisListType
BF16NP = ml_dtypes.bfloat16

B, S, D = 4, 4096, 4096
H, KVH, G, DH = 32, 8, 4, 128
CB, NSEL, WIN = 16, 4, 64
INTER, RANK = 11008, 1024
NCORE = 8
TL, NT, NBLK = 2048, 16, 128
SCALE = DH ** -0.5
SHIFT = 18.0
NEGBIG = -1.0e30
LN1E10 = -23.025850929940457


ENGINES = ("sync", "gpsimd", "tensor", "vector", "scalar")

class Sched:
    AUTO = ("vector", "scalar")

    def __init__(self, nc):
        self.nc = nc
        self.steps = {e: [] for e in ENGINES}
        self.counts = {}
        self.selfsem = {e: nc.alloc_semaphore(name=f"self_{e}") for e in self.AUTO}
        self.last_sig = {e: None for e in self.AUTO}
        self.slots = {}

    def val(self, sem):
        return self.counts.get(id(sem), 0)

    def bump(self, sem, n):
        v = self.val(sem) + n
        self.counts[id(sem)] = v
        return v

    def step(self, eng, fn, waits=(), inc=None, inc_n=1):
        w = [(s, v) for (s, v) in waits if v > 0]
        incs = []
        if eng in self.AUTO:
            if self.last_sig[eng] is not None:
                w.append(self.last_sig[eng])
            if inc is not None:
                ret = self.bump(inc, inc_n)
                incs.append((inc, inc_n))
                self.last_sig[eng] = (inc, ret)
            else:
                ss = self.selfsem[eng]
                v = self.bump(ss, 1)
                incs.append((ss, 1))
                self.last_sig[eng] = (ss, v)
                ret = None
        else:
            if inc is not None:
                ret = self.bump(inc, inc_n)
                incs.append((inc, inc_n))
            else:
                ret = None
        self.steps[eng].append((fn, w, incs))
        return ret

    def slot_sem(self, key):
        if key not in self.slots:
            self.slots[key] = self.nc.alloc_semaphore(name=f"dq_{key}")
        return self.slots[key]

    def dma(self, eng, out, in_, slot, waits=(), transpose=False):
        sem = self.slot_sem(slot)
        w = [(s, v) for (s, v) in waits if v > 0]
        prior = self.val(sem)
        if prior > 0:
            w.append((sem, prior))
        if transpose:
            fn = lambda e, o=out, i=in_: e.dma_start_transpose(o, i)
        else:
            fn = lambda e, o=out, i=in_: e.dma_start(out=o, in_=i)
        v = self.bump(sem, 16)
        self.steps[eng].append((fn, w, [(sem, 16)]))
        return (sem, v)

    def dma_group(self, eng, pairs, slot, waits=(), transpose=False):
        sem = self.slot_sem(slot)
        w = [(s, v) for (s, v) in waits if v > 0]
        prior = self.val(sem)
        if prior > 0:
            w.append((sem, prior))
        first = True
        for out, in_ in pairs:
            if transpose:
                fn = lambda e, o=out, i=in_: e.dma_start_transpose(o, i)
            else:
                fn = lambda e, o=out, i=in_: e.dma_start(out=o, in_=i)
            v = self.bump(sem, 16)
            self.steps[eng].append((fn, w if first else [], [(sem, 16)]))
            first = False
        return (sem, self.val(sem))

    def emit(self):
        with self.nc.Block() as blk:
            def runner(eng):
                def _(e):
                    for fn, waits, incs in self.steps[eng]:
                        for s, v in waits:
                            e.wait_ge(s, v)
                        ins = fn(e)
                        for s, n in incs:
                            assert ins is not None, "step with inc returned no instruction"
                            ins.then_inc(s, n)
                return _
            blk.sync(runner("sync"))
            blk.gpsimd(runner("gpsimd"))
            blk.tensor(runner("tensor"))
            blk.vector(runner("vector"))
            blk.scalar(runner("scalar"))


CONST_VALS = [0.0, 1.0, -1.0, 0.5, -0.5, 2.0, -2.0, 1e-5, -SHIFT, SCALE,
              SCALE / 4.0, -1000.0, NEGBIG, 1.0 / D, LN1E10, -1.0 / D]

class Prog:
    def __init__(self, name):
        self.nc = bass.Bass("TRN2", target_bir_lowering=False, name=name)
        self.ins, self.outs = {}, {}
        self.sched = Sched(self.nc)

    def inp(self, name, shape, dtype=F32):
        ap = self.nc.dram_tensor(name, list(shape), dtype, kind="ExternalInput").ap()
        self.ins[name] = ap
        return ap

    def out(self, name, shape, dtype=F32):
        ap = self.nc.dram_tensor(name, list(shape), dtype, kind="ExternalOutput").ap()
        self.outs[name] = ap
        return ap

    def sb(self, name, shape, dtype):
        return self.nc.alloc_sbuf_tensor(name, list(shape), dtype)

    def ps(self, name, shape, dtype=F32):
        return self.nc.alloc_psum_tensor(name, list(shape), dtype)

    def sem(self, name):
        return self.nc.alloc_semaphore(name=name)

    def add_consts(self):
        self.const_in = self.inp("consts_", [128, len(CONST_VALS)], F32)
        self.const_sb = self.sb("consts_sb", [128, len(CONST_VALS)], F32)
        for i, v in enumerate(CONST_VALS):
            self.nc.const_aps.register(F32, float(v), self.const_sb[:, i:i + 1])
        self.const_ready = self.sched.dma("gpsimd", self.const_sb[:], self.const_in, "consts")

    def finish(self, final_waits):
        def fn(e):
            return None
        self.sched.step("gpsimd", fn, waits=final_waits)
        self.sched.emit()

def consts_np():
    return np.tile(np.asarray(CONST_VALS, np.float32)[None, :], (128, 1))


def _gpos(par):
    l = np.arange(TL)
    return (2 * (l // 128) + par) * 128 + (l % 128)

def _kappa_pos(par):
    return np.concatenate([_gpos(par), _gpos(1 - par)])

def _chat_block(par):
    out = np.zeros(256, np.int64)
    for half, pp in enumerate((par, 1 - par)):
        c = np.arange(128)
        out[half * 128 + c] = 8 * (2 * (c // 8) + pp) + c % 8
    return out

def alibi_slopes():
    import math
    n = H
    cp2 = 2 ** math.floor(math.log2(n))
    base = 2.0 ** (-2.0 ** (-(math.log2(cp2) - 3)))
    s = [base ** i for i in range(1, cp2 + 1)]
    if cp2 != n:
        eb = 2.0 ** (-2.0 ** (-(math.log2(2 * cp2) - 3)))
        ne = min(cp2, n - cp2)
        s += [eb ** i for i in range(1, 2 * ne + 1, 2)]
    return np.asarray(s, np.float64)

def host_tables(par):
    t = {}
    g = _gpos(par)
    cb = _chat_block(par)
    inv = 1.0 / 10000.0 ** (np.arange(0, DH, 2, dtype=np.float64) / DH)
    invd = np.repeat(inv, 2)
    ang = invd[:, None] * g[None, :].astype(np.float64)
    t["COST"] = np.cos(ang).astype(np.float32)
    t["SINT"] = np.sin(ang).astype(np.float32)
    R = np.zeros((DH, DH), np.float32)
    for f in range(DH // 2):
        R[2 * f, 2 * f + 1] = -1.0
        R[2 * f + 1, 2 * f] = 1.0
    t["RT"] = np.ascontiguousarray(R.T)
    vis = (16 * cb[None, :] + 15) < g[:, None]
    t["CMADD"] = np.where(vis, 0.0, NEGBIG).astype(np.float32)
    t["CM01T"] = np.ascontiguousarray(vis.T.astype(np.float32))
    k = np.arange(128)
    E = np.zeros((128, 16 * 128), np.float32)
    for kt in range(16):
        E[:, kt * 128 + k] = (np.arange(128)[:, None] == 8 * kt + (k // 16)[None, :])
    t["EWIDE"] = E
    q = np.arange(128)
    t["OWND"] = (((k[:, None] // 16) == (q[None, :] // 16)) & (k[:, None] <= q[None, :])).astype(np.float32)
    sl = np.zeros((3, 128, 128), np.float32)
    sl[0] = ((k[:, None] <= q[None, :]) & (k[:, None] >= q[None, :] - WIN))
    for slot, dk in ((1, -1), (2, 0)):
        dt_ = par - (2 * dk + (1 - par))
        qg_m_kg = dt_ * 128 + q[None, :] - k[:, None]
        sl[slot] = ((qg_m_kg > 0) & (qg_m_kg <= WIN))
    t["SLID"] = sl
    t["IDENT"] = np.eye(128, dtype=np.float32)
    return t

def host_alibi(par):
    g = _gpos(par)
    kp = _kappa_pos(par)
    sl = alibi_slopes()
    A = np.zeros((NT, 32, 128, H), np.float32)
    for qt in range(NT):
        anchor = g[qt * 128 + 127]
        for kt in range(32):
            kpos = kp[kt * 128:(kt + 1) * 128].astype(np.float64)
            val = np.exp(sl[None, :] * (kpos[:, None] - anchor))
            val[kpos > anchor, :] = 0.0
            A[qt, kt] = val.astype(np.float32)
    k = np.arange(128)[:, None]
    q = np.arange(128)[None, :]
    base = {0: 0, 1: (-128 if par == 0 else -384), 2: (128 if par == 0 else -128)}
    ADDM = np.zeros((3, H, 128, 128), np.float32)
    for slot in range(3):
        rel = (k + base[slot]) - q
        for h in range(H):
            m = sl[h] * rel
            m[rel > 0] = NEGBIG
            ADDM[slot, h] = m
    j = np.arange(128)
    DTAB = np.minimum(np.exp(sl[None, :] * (127 - j)[:, None]), 1e30).astype(np.float32)
    return A, ADDM, DTAB

def host_lhsT(WT):
    K, M = WT.shape
    return np.ascontiguousarray(
        WT.reshape(K // 128, 128, M // 128, 128).transpose(1, 2, 0, 3))


def build_ln(name, with_add, final_out_f32=False):
    p = Prog(name)
    nc, S_ = p.nc, p.sched
    X = p.inp("X", [TL, D], F32)
    if with_add:
        YT = p.inp("YT", [D, TL], BF16)
        XOUT = p.out("XOUT", [TL, D], F32)
    do_ln = not final_out_f32
    if do_ln:
        XLN = p.out("XLN", [TL, D], BF16)
    p.add_consts()

    xt = [p.sb(f"xt{i}", [128, D], F32) for i in range(2)]
    yt = [p.sb(f"yt{i}", [128, D], BF16) for i in range(2)] if with_add else None
    xlt = [p.sb(f"xlt{i}", [128, D], BF16) for i in range(2)]
    sq = p.sb("sq", [128, D], F32)
    stt = p.sb("stt", [128, 8], F32)
    vs = p.sem("vs"); ss = p.sem("ss")

    tile_free = [(), ()]
    prev_apply = None
    out_tokens = []
    for t in range(NT):
        bi = t % 2
        lx = S_.dma("sync", xt[bi][:], X[t * 128:(t + 1) * 128, :], f"x{bi}",
                    waits=tile_free[bi])
        if with_add:
            ty_ = S_.dma("sync", yt[bi][:], YT[:, t * 128:(t + 1) * 128], f"y{bi}",
                         transpose=True, waits=tile_free[bi])
            lv2 = S_.step("vector",
                          lambda e, bi=bi: e.tensor_add(xt[bi][:], xt[bi][:], yt[bi][:]),
                          waits=[lx, ty_], inc=vs)
            xready = [(vs, lv2)]
            so = S_.dma("gpsimd", XOUT[t * 128:(t + 1) * 128, :], xt[bi][:], f"xo{bi}",
                        waits=xready)
            out_tokens.append(so)
        else:
            xready = [lx]
            so = None
        if not do_ln:
            tile_free[bi] = (so,)
            continue
        war = [prev_apply] if prev_apply is not None else []
        sv = S_.step("scalar",
                     lambda e, bi=bi: e.activation(out=sq[:], in_=xt[bi][:],
                                                   func=AF.Square, accum_out=stt[:, 2:3]),
                     waits=xready + war, inc=ss)
        S_.step("vector", lambda e, bi=bi: e.reduce_sum(out=stt[:, 0:1], in_=xt[bi][:], axis=AX.X),
                waits=xready)
        S_.step("vector", lambda e: e.tensor_scalar_mul(stt[:, 1:2], stt[:, 0:1], 1.0 / D))
        S_.step("vector", lambda e: e.tensor_tensor(out=stt[:, 5:6], in0=stt[:, 1:2], in1=stt[:, 1:2], op=ALU.mult))
        S_.step("vector", lambda e: e.tensor_scalar(out=stt[:, 3:4], in0=stt[:, 2:3], scalar1=1.0 / D,
                                                    scalar2=None, op0=ALU.mult),
                waits=[(ss, sv)])
        vv2 = S_.step("vector", lambda e: e.tensor_sub(stt[:, 3:4], stt[:, 3:4], stt[:, 5:6]), inc=vs)
        S_.step("scalar", lambda e: e.activation(out=stt[:, 6:7], in_=stt[:, 3:4], func=AF.Ln, bias=1e-5),
                waits=[(vs, vv2), p.const_ready])
        sv2 = S_.step("scalar", lambda e: e.activation(out=stt[:, 4:5], in_=stt[:, 6:7], func=AF.Exp, scale=-0.5),
                      inc=ss)
        vv3 = S_.step("vector",
                      lambda e, bi=bi: e.tensor_scalar(out=xlt[bi][:], in0=xt[bi][:],
                                                       scalar1=stt[:, 1:2], scalar2=stt[:, 4:5],
                                                       op0=ALU.subtract, op1=ALU.mult),
                      waits=[(ss, sv2)], inc=vs)
        prev_apply = (vs, vv3)
        sl = S_.dma("gpsimd", XLN[t * 128:(t + 1) * 128, :], xlt[bi][:], f"xl{bi}",
                    waits=[(vs, vv3)])
        out_tokens.append(sl)
        tile_free[bi] = ((so,) if so else ()) + ((vs, vv3),) + (sl,)
        tile_free[bi] = tuple(w for w in tile_free[bi] if w)

    p.finish(out_tokens[-4:])
    return p


def build_l0proj(name="l0proj"):
    p = Prog(name)
    nc, S_ = p.nc, p.sched
    XLN = p.inp("XLN", [TL, D], BF16)
    W2 = p.inp("W2", [128, 48 * 32 * 128], F32)
    WG2 = p.inp("WG2", [128, 32 * 96], F32)
    BG = p.inp("BG", [128, 96], F32)
    KPT = p.inp("KPT", [128, KVH * CB], F32)
    VPT = p.inp("VPT", [128, KVH * CB], F32)
    COST = p.inp("COST", [DH, TL], F32)
    SINT = p.inp("SINT", [DH, TL], F32)
    RT = p.inp("RT", [DH, DH], F32)

    QT = p.out("QT", [D, TL], BF16)
    RQT = p.out("RQT", [D, TL], BF16)
    QBART = p.out("QBART", [KVH * DH, TL], F32)
    RKT = p.out("RKT", [KVH * DH, TL], BF16)
    VT = p.out("VT", [KVH * DH, TL], BF16)
    KPK = p.out("KPK", [KVH * DH, TL], BF16)
    VPK = p.out("VPK", [KVH * DH, TL], BF16)
    GATES = p.out("GATES", [TL, 96], F32)
    p.add_consts()

    xlt = p.sb("xlt", [128, 32 * TL], BF16)
    wti = [p.sb(f"wt{i}", [128, 32 * 128], BF16) for i in range(2)]
    cossb = p.sb("cossb", [128, TL], BF16)
    sinsb = p.sb("sinsb", [128, TL], BF16)
    rtsb = p.sb("rtsb", [128, DH], BF16)
    qsb = [p.sb(f"qsb{i}", [128, TL], BF16) for i in range(2)]
    rsb = [p.sb(f"rsb{i}", [128, TL], BF16) for i in range(2)]
    t2sb = p.sb("t2sb", [128, TL], BF16)
    t3sb = p.sb("t3sb", [128, TL], BF16)
    qb32 = p.sb("qb32", [128, TL], F32)
    pksb = [p.sb(f"pksb{i}", [128, TL], BF16) for i in range(2)]
    kptsb = p.sb("kptsb", [128, KVH * CB], BF16)
    vptsb = p.sb("vptsb", [128, KVH * CB], BF16)
    wgsb = p.sb("wgsb", [128, 32 * 96], BF16)
    bgsb = p.sb("bgsb", [128, 96], F32)
    gsb = p.sb("gsb", [128, 96], F32)

    pA = p.ps("pA", [128, TL])
    pB = p.ps("pB", [128, TL])

    pe = p.sem("pe"); ev = p.sem("ev"); rp = p.sem("rp")

    tab_ready = S_.dma_group("gpsimd",
                             [(cossb[:], COST), (sinsb[:], SINT), (rtsb[:], RT),
                              (kptsb[:], KPT), (vptsb[:], VPT), (wgsb[:], WG2), (bgsb[:], BG)],
                             "tabs")
    xl_all = S_.dma_group("sync",
                          [(xlt[:, f * TL:(f + 1) * TL], XLN[:, f * 128:(f + 1) * 128])
                           for f in range(32)], "xlt", transpose=True)

    evict_done = [None] * 48
    prev_ev = None
    buf_free = {}
    rsb_free = {}
    pk_free = {}
    qsb_read = {}
    prev_rv = None
    prev_rpv = {}
    out_tokens = []
    for o in range(48):
        wtok = S_.dma("gpsimd", wti[o % 2][:], W2[:, o * 4096:(o + 1) * 4096], f"w{o % 2}",
                      waits=[evict_done[o - 2]] if o >= 2 else ())

        def mmfn(e, o=o):
            ins = None
            for f in range(32):
                for ch in range(4):
                    ins = e.matmul(pA[:, ch * 512:(ch + 1) * 512],
                                   wti[o % 2][:, f * 128:(f + 1) * 128],
                                   xlt[:, f * TL + ch * 512: f * TL + (ch + 1) * 512],
                                   start=(f == 0), stop=(f == 31))
            return ins
        mw = [wtok, xl_all]
        if prev_ev is not None:
            mw.append(prev_ev)
        pv = S_.step("tensor", mmfn, waits=mw, inc=pe)

        evw = [(pe, pv)] + list(buf_free.get(o - 2, ()))
        evv = S_.step("scalar", lambda e, o=o: e.copy(out=qsb[o % 2][:], in_=pA[:]),
                      waits=evw, inc=ev)
        prev_ev = (ev, evv)
        evict_done[o] = (ev, evv)

        frees = []
        if o < 40:
            def rotfn(e, o=o):
                sap = qsb[o % 2][:]
                ins = None
                for ch in range(4):
                    ins = e.matmul(pB[:, ch * 512:(ch + 1) * 512], rtsb[:],
                                   sap[:, ch * 512:(ch + 1) * 512], start=True, stop=True)
                return ins
            rw = [(ev, evv), tab_ready]
            if prev_rv is not None:
                rw.append(prev_rv)
            rpv = S_.step("tensor", rotfn, waits=rw, inc=pe)
            frees.append((pe, rpv))
            S_.step("vector", lambda e, o=o: e.tensor_tensor(
                out=t2sb[:], in0=qsb[o % 2][:], in1=cossb[:], op=ALU.mult),
                waits=[(ev, evv), tab_ready])
            S_.step("vector", lambda e: e.tensor_tensor(out=t3sb[:], in0=pB[:], in1=sinsb[:], op=ALU.mult),
                    waits=[(pe, rpv)])
            rv = S_.step("vector", lambda e, o=o: e.tensor_add(rsb[o % 2][:], t2sb[:], t3sb[:]),
                         waits=list(rsb_free.get(o - 2, ())), inc=rp)
            prev_rv = (rp, rv)
        if o < 32:
            sq_ = S_.dma("sync", QT[o * 128:(o + 1) * 128, :], qsb[o % 2][:], f"qt{o % 2}",
                         waits=[(ev, evv)])
            sr = S_.dma("sync", RQT[o * 128:(o + 1) * 128, :], rsb[o % 2][:], f"rq{o % 2}",
                        waits=[(rp, rv)])
            rsb_free[o] = (sr,)
            out_tokens += [sq_, sr]
            frees.append(sq_)
            g = o % 4
            if g == 0:
                w = [(ev, evv)] + list(qsb_read.get("qbst", ()))
                S_.step("vector", lambda e, o=o: e.tensor_copy(out=qb32[:], in_=qsb[o % 2][:]), waits=w)
            else:
                S_.step("vector", lambda e, o=o: e.tensor_add(qb32[:], qb32[:], qsb[o % 2][:]),
                        waits=[(ev, evv)])
            frees.append(S_.last_sig["vector"])
            if g == 3:
                sig = S_.last_sig["vector"]
                dqb = S_.dma("sync", QBART[(o // 4) * 128:(o // 4 + 1) * 128, :], qb32[:],
                             "qbst", waits=[sig])
                qsb_read["qbst"] = (dqb,)
                out_tokens.append(dqb)
        elif o < 40:
            h = o - 32
            sr = S_.dma("sync", RKT[h * 128:(h + 1) * 128, :], rsb[o % 2][:], f"rq{o % 2}",
                        waits=[(rp, rv)])
            rsb_free[o] = (sr,)
            out_tokens.append(sr)
            pw = [(ev, evv), tab_ready] + list(pk_free.get(o - 2, ()))
            S_.step("vector", lambda e, o=o, h=h: e.tensor_tensor(
                out=pksb[o % 2][:].rearrange("p (c j) -> p c j", j=16),
                in0=qsb[o % 2][:].rearrange("p (c j) -> p c j", j=16),
                in1=kptsb[:, h * CB:(h + 1) * CB].unsqueeze(1).broadcast_to([128, 128, 16]),
                op=ALU.add), waits=pw)
            sig = S_.last_sig["vector"]
            frees.append(sig)
            dpk = S_.dma("sync", KPK[h * 128:(h + 1) * 128, :], pksb[o % 2][:], f"pk{o % 2}",
                         waits=[sig])
            pk_free[o] = (dpk,)
            out_tokens.append(dpk)
        else:
            h = o - 40
            sv_ = S_.dma("sync", VT[h * 128:(h + 1) * 128, :], qsb[o % 2][:], f"qt{o % 2}",
                         waits=[(ev, evv)])
            frees.append(sv_)
            out_tokens.append(sv_)
            pw = [(ev, evv), tab_ready] + list(pk_free.get(o - 2, ()))
            S_.step("vector", lambda e, o=o, h=h: e.tensor_tensor(
                out=pksb[o % 2][:].rearrange("p (c j) -> p c j", j=16),
                in0=qsb[o % 2][:].rearrange("p (c j) -> p c j", j=16),
                in1=vptsb[:, h * CB:(h + 1) * CB].unsqueeze(1).broadcast_to([128, 128, 16]),
                op=ALU.add), waits=pw)
            sig = S_.last_sig["vector"]
            frees.append(sig)
            dpk = S_.dma("sync", VPK[h * 128:(h + 1) * 128, :], pksb[o % 2][:], f"pk{o % 2}",
                         waits=[sig])
            pk_free[o] = (dpk,)
            out_tokens.append(dpk)
        buf_free[o] = tuple(frees)

    g_prev = None
    for t in range(NT):
        def gfn(e, t=t):
            ins = None
            for f in range(32):
                ins = e.matmul(pB[:, 1024:1120],
                               xlt[:, f * TL + t * 128: f * TL + (t + 1) * 128],
                               wgsb[:, f * 96:(f + 1) * 96],
                               start=(f == 0), stop=(f == 31))
            return ins
        w = [tab_ready, xl_all] + ([g_prev] if g_prev else [(rp, S_.val(rp))])
        pv = S_.step("tensor", gfn, waits=w, inc=pe)
        S_.step("vector", lambda e: e.tensor_tensor(out=pB[:, 1024:1120], in0=pB[:, 1024:1120],
                                                    in1=bgsb[:], op=ALU.add),
                waits=[(pe, pv)])
        sig = S_.last_sig["vector"]
        sv = S_.step("scalar", lambda e: e.activation(out=gsb[:], in_=pB[:, 1024:1120], func=AF.Sigmoid),
                     waits=[sig], inc=ev)
        dg = S_.dma("sync", GATES[t * 128:(t + 1) * 128, :], gsb[:], "gst", waits=[(ev, sv)])
        g_prev = dg
        out_tokens.append(dg)

    lasts = {}
    for tok in out_tokens:
        lasts[id(tok[0])] = tok
    p.finish(list(lasts.values()))
    return p


def build_l0compress(name="l0comp"):
    p = Prog(name)
    nc, S_ = p.nc, p.sched
    KPK = p.inp("KPK", [KVH * DH, TL], BF16)
    VPK = p.inp("VPK", [KVH * DH, TL], BF16)
    KW1T2 = p.inp("KW1T2", [128, 16 * 16 * 128], F32)
    VW1T2 = p.inp("VW1T2", [128, 16 * 16 * 128], F32)
    KW2T2 = p.inp("KW2T2", [128, 16 * 128], F32)
    VW2R = p.inp("VW2R", [128, 16 * 128], F32)
    KB1 = p.inp("KB1", [128, 16], F32)
    VB1 = p.inp("VB1", [128, 16], F32)
    KB2 = p.inp("KB2", [128, 1], F32)
    VB2T = p.inp("VB2T", [128, 128], F32)
    CKT = p.out("CKT", [DH, KVH * NBLK], F32)
    CV = p.out("CV", [KVH * NBLK, DH], BF16)
    p.add_consts()

    kw1sb = p.sb("kw1sb", [128, 16 * 16 * 128], BF16)
    vw1sb = p.sb("vw1sb", [128, 16 * 16 * 128], BF16)
    kw2sb = p.sb("kw2sb", [128, 16 * 128], BF16)
    vw2sb = p.sb("vw2sb", [128, 16 * 128], BF16)
    kb1sb = p.sb("kb1sb", [128, 16], F32)
    vb1sb = p.sb("vb1sb", [128, 16], F32)
    kb2sb = p.sb("kb2sb", [128, 1], F32)
    vb2tsb = p.sb("vb2tsb", [128, 128], F32)
    pk = [p.sb(f"pk{i}", [128, TL], BF16) for i in range(2)]
    h1sb = p.sb("h1sb", [128, 16 * 128], BF16)
    cktf = p.sb("cktf", [128, KVH * NBLK], F32)
    cvsb = p.sb("cvsb", [128, DH], BF16)
    pA = p.ps("pA", [128, TL])

    pe = p.sem("pe"); ev = p.sem("ev")

    wv = S_.dma_group("gpsimd",
                      [(kw1sb[:], KW1T2), (vw1sb[:], VW1T2), (kw2sb[:], KW2T2),
                       (vw2sb[:], VW2R), (kb1sb[:], KB1), (vb1sb[:], VB1),
                       (kb2sb[:], KB2), (vb2tsb[:], VB2T)], "cw")

    cv_prev = None
    h1_last_pe = {}
    evict_sig = {}
    out_tokens = []
    for kv in range(2):
        SRC = KPK if kv == 0 else VPK
        w1sb = kw1sb if kv == 0 else vw1sb
        b1sb = kb1sb if kv == 0 else vb1sb
        for h in range(KVH):
            i = kv * KVH + h
            bi = i % 2
            lw = [(pe, h1_last_pe[i - 2])] if i >= 2 else []
            lv = S_.dma("gpsimd", pk[bi][:], SRC[h * 128:(h + 1) * 128, :], f"pk{bi}", waits=lw)
            h1_sigs = []
            for m in range(16):
                def h1fn(e, m=m, bi=bi, w1sb=w1sb):
                    ins = None
                    for j in range(16):
                        rhs = pk[bi][:].rearrange("p (c j) -> p c j", j=16)[:, :, j]
                        ins = e.matmul(pA[:, 1024 + (m % 2) * 512: 1152 + (m % 2) * 512],
                                       w1sb[:, (m * 16 + j) * 128:(m * 16 + j + 1) * 128],
                                       rhs, start=(j == 0), stop=(j == 15))
                    return ins
                w = [lv, wv]
                if m >= 2:
                    w.append(h1_sigs[m - 2])
                pv = S_.step("tensor", h1fn, waits=w, inc=pe)
                sv = S_.step("scalar", lambda e, m=m, b1sb=b1sb: e.activation(
                    out=h1sb[:, m * 128:(m + 1) * 128], in_=pA[:, 1024 + (m % 2) * 512: 1152 + (m % 2) * 512],
                    func=AF.Relu, bias=b1sb[:, m:m + 1]), waits=[(pe, pv)], inc=ev)
                h1_sigs.append((ev, sv))
            if kv == 0:
                def cktfn(e):
                    ins = None
                    for m in range(16):
                        ins = e.matmul(pA[:, 0:128], kw2sb[:, m * 128:(m + 1) * 128],
                                       h1sb[:, m * 128:(m + 1) * 128],
                                       start=(m == 0), stop=(m == 15))
                    return ins
                w = [h1_sigs[-1]] + ([evict_sig[i - 1]] if (i - 1) in evict_sig else [])
                pv = S_.step("tensor", cktfn, waits=w, inc=pe)
                h1_last_pe[i] = pv
                sv = S_.step("scalar", lambda e, h=h: e.activation(
                    out=cktf[:, h * 128:(h + 1) * 128], in_=pA[:, 0:128],
                    func=AF.Identity, bias=kb2sb[:, 0:1]), waits=[(pe, pv)], inc=ev)
                evict_sig[i] = (ev, sv)
            else:
                def cvfn(e):
                    ins = None
                    for m in range(16):
                        ins = e.matmul(pA[:, 512:640],
                                       h1sb[:, m * 128:(m + 1) * 128],
                                       vw2sb[:, m * 128:(m + 1) * 128],
                                       start=(m == 0), stop=(m == 15))
                    return ins
                w = [h1_sigs[-1]] + ([evict_sig[i - 1]] if (i - 1) in evict_sig else []) + ([cv_prev] if cv_prev else [])
                pv = S_.step("tensor", cvfn, waits=w, inc=pe)
                h1_last_pe[i] = pv
                S_.step("vector", lambda e: e.tensor_tensor(out=cvsb[:], in0=pA[:, 512:640],
                                                            in1=vb2tsb[:], op=ALU.add),
                        waits=[(pe, pv), wv])
                sig = S_.last_sig["vector"]
                evict_sig[i] = sig
                dv = S_.dma("sync", CV[h * 128:(h + 1) * 128, :], cvsb[:], "cvst", waits=[sig])
                cv_prev = dv
                out_tokens.append(dv)
    dk = S_.dma("sync", CKT, cktf[:], "ckst", waits=[evict_sig[7]])
    out_tokens.append(dk)
    p.finish([out_tokens[-2], out_tokens[-1]])
    return p


def build_l0attn(name="l0attn", kvh_list=None, qt_list=None, debug=False, no_slid=False, skip_fine=(), skip_comp=()):
    kvhs = kvh_list if kvh_list is not None else list(range(KVH))
    qts = qt_list if qt_list is not None else list(range(NT))
    p = Prog(name)
    nc, S_ = p.nc, p.sched
    QT = p.inp("QT", [D, TL], BF16)
    RQT = p.inp("RQT", [D, TL], BF16)
    QBART = p.inp("QBART", [KVH * DH, TL], F32)
    RKT0 = p.inp("RKT0", [KVH * DH, TL], BF16)
    RKT1 = p.inp("RKT1", [KVH * DH, TL], BF16)
    VT0 = p.inp("VT0", [KVH * DH, TL], BF16)
    VT1 = p.inp("VT1", [KVH * DH, TL], BF16)
    CKT0 = p.inp("CKT0", [DH, KVH * NBLK], F32)
    CKT1 = p.inp("CKT1", [DH, KVH * NBLK], F32)
    CV0 = p.inp("CV0", [KVH * NBLK, DH], BF16)
    CV1 = p.inp("CV1", [KVH * NBLK, DH], BF16)
    MEMKT = p.inp("MEMKT", [DH, KVH], F32)
    MEMV = p.inp("MEMV", [KVH, DH], F32)
    GATES = p.inp("GATES", [TL, 96], F32)
    CMADD = p.inp("CMADD", [TL, 256], F32)
    CM01T = p.inp("CM01T", [256, TL], F32)
    EWIDE = p.inp("EWIDE", [128, 16 * 128], F32)
    OWND = p.inp("OWND", [128, 128], F32)
    SLID = p.inp("SLID", [3 * 128, 128], F32)
    IDENT = p.inp("IDENT", [128, 128], F32)
    AOUT = p.out("AOUT", [TL, D], BF16)
    if debug:
        DBGC = p.out("DBGC", [TL, D], F32)
        DBGF = p.out("DBGF", [TL, D], F32)
        DBGS = p.out("DBGS", [TL, D], F32)
        DBGZ = p.out("DBGZ", [TL, 24], F32)
        dbg_sb = None
    p.add_consts()

    qt_sb = p.sb("qt_sb", [128, G * TL], BF16)
    rqt_sb = p.sb("rqt_sb", [128, G * TL], BF16)
    rk_sb = p.sb("rk_sb", [128, 2 * TL], BF16)
    v_sb = p.sb("v_sb", [128, 32 * 144], BF16)
    ckbf = p.sb("ckbf", [128, 256], BF16)
    ckf32 = p.sb("ckf32", [128, 256], F32)
    cvA = p.sb("cvA", [128, 128], BF16)
    cvB = p.sb("cvB", [128, 128], BF16)
    memkt_sb = p.sb("memkt_sb", [128, KVH], BF16)
    memv_sb = p.sb("memv_sb", [1, 128], BF16)
    qbart_sb = p.sb("qbart_sb", [128, TL], F32)
    gates_sb = p.sb("gates_sb", [128, NT * 96 + 8], F32)
    cmadd_sb = p.sb("cmadd_sb", [128, NT * 256], F32)
    cm01_sb = p.sb("cm01_sb", [128, 2 * TL], BF16)
    ew_sb = p.sb("ew_sb", [128, 16 * 128], BF16)
    own_sb = p.sb("own_sb", [128, 128], BF16)
    slid_sb = p.sb("slid_sb", [128, 3 * 128], BF16)
    id_sb = p.sb("id_sb", [128, 128], BF16)
    ones_sb = p.sb("ones_sb", [128, 8], BF16)
    imp_sb = p.sb("imp_sb", [128, NT * 256], F32)
    scr = p.sb("scr", [128, NT * 256], F32)
    scr2 = p.sb("scr2", [128, 256], F32)
    selst = p.sb("selst", [128, 16 * 8], F32)
    mb_sb = p.sb("mb_sb", [128, NT * 256], BF16)
    mbt_sb = p.sb("mbt_sb", [128, 256], BF16)
    pcA_sb = p.sb("pcA_sb", [128, 512], BF16)
    pcB_sb = p.sb("pcB_sb", [128, 512], BF16)
    pcm_sb = p.sb("pcm_sb", [1, 512], BF16)
    pf_sb = [p.sb(f"pf_sb{i}", [128, 512], BF16) for i in range(2)]
    pfm_sb = [p.sb(f"pfm_sb{i}", [128, 512], BF16) for i in range(2)]
    psm_sb = p.sb("psm_sb", [128, 512], BF16)
    m16b_sb = p.sb("m16b_sb", [128, 128], BF16)
    c_un = p.sb("c_un", [128, 512], F32)
    rz_sb = p.sb("rz_sb", [128, 16], F32)
    e_sb = p.sb("e_sb", [128, 12], F32)
    acc_sb = p.sb("acc_sb", [128, 128], F32)
    aout_sb = [p.sb(f"aout_sb{i}", [128, 512], BF16) for i in range(2)]
    if debug:
        dbg_sb = p.sb("dbg_sb", [128, 3 * 512], F32)
        dbgz_sb = p.sb("dbgz_sb", [128, 24], F32)

    pA = p.ps("pA", [128, 1536])
    pB = p.ps("pB", [128, TL])
    pT = p.ps("pT", [128, 256], BF16)

    pe = p.sem("pe"); ev = p.sem("ev"); evs = p.sem("evs")

    ONES_IDX = 1
    vset = S_.step("vector", lambda e: e.memset(v_sb[:], 1.0))
    S_.step("vector", lambda e: e.memset(ones_sb[:], 1.0))
    S_.step("vector", lambda e: e.memset(memv_sb[:], 1.0))
    ones_ready = S_.last_sig["vector"]
    tab_pairs = ([(gates_sb[:, t * 96:(t + 1) * 96], GATES[t * 128:(t + 1) * 128, :])
                  for t in range(NT)] +
                 [(cmadd_sb[:, t * 256:(t + 1) * 256], CMADD[t * 128:(t + 1) * 128, :])
                  for t in range(NT)] +
                 [(cm01_sb[:, 0:TL], CM01T[0:128, :]),
                  (cm01_sb[:, TL:2 * TL], CM01T[128:256, :]),
                  (ew_sb[:], EWIDE), (own_sb[:], OWND)] +
                 [(slid_sb[:, sl_ * 128:(sl_ + 1) * 128], SLID[sl_ * 128:(sl_ + 1) * 128, :])
                  for sl_ in range(3)] +
                 [(id_sb[:], IDENT), (memkt_sb[:], MEMKT)])
    tabs = S_.dma_group("gpsimd", tab_pairs, "tabs")

    out_tokens = []
    kvh_loads_prev = ()
    ao_tok = {}
    for kvh in kvhs:
        lq = S_.dma_group("sync",
                          [(qt_sb[:, g * TL:(g + 1) * TL],
                            QT[(kvh * G + g) * 128:(kvh * G + g + 1) * 128, :]) for g in range(G)] +
                          [(rqt_sb[:, g * TL:(g + 1) * TL],
                            RQT[(kvh * G + g) * 128:(kvh * G + g + 1) * 128, :]) for g in range(G)] +
                          [(rk_sb[:, 0:TL], RKT0[kvh * 128:(kvh + 1) * 128, :]),
                           (rk_sb[:, TL:2 * TL], RKT1[kvh * 128:(kvh + 1) * 128, :]),
                           (qbart_sb[:], QBART[kvh * 128:(kvh + 1) * 128, :])],
                          "kvq", waits=kvh_loads_prev)
        lv = S_.dma_group("sync",
                          [(v_sb[:, kt * 144: kt * 144 + 128],
                            (VT0 if kt < 16 else VT1)[kvh * 128:(kvh + 1) * 128,
                                                      (kt % 16) * 128:((kt % 16) + 1) * 128])
                           for kt in range(32)],
                          "kvv", waits=[ones_ready] + list(kvh_loads_prev), transpose=True)
        lc = S_.dma_group("gpsimd",
                          [(ckbf[:, 0:128], CKT0[:, kvh * 128:(kvh + 1) * 128]),
                           (ckbf[:, 128:256], CKT1[:, kvh * 128:(kvh + 1) * 128]),
                           (ckf32[:, 0:128], CKT0[:, kvh * 128:(kvh + 1) * 128]),
                           (ckf32[:, 128:256], CKT1[:, kvh * 128:(kvh + 1) * 128]),
                           (cvA[:], CV0[kvh * 128:(kvh + 1) * 128, :]),
                           (cvB[:], CV1[kvh * 128:(kvh + 1) * 128, :]),
                           (memv_sb[0:1, 0:128], MEMV[kvh:kvh + 1, :])],
                          "kvc", waits=[ones_ready] + list(kvh_loads_prev))

        imp_evs = []
        for i, qt in enumerate(qts):
            pv = S_.step("tensor", lambda e, qt=qt: e.matmul(
                pA[:, 0:256], qbart_sb[:, qt * 128:(qt + 1) * 128], ckf32[:],
                start=True, stop=True),
                waits=[lq, lc, tabs] + (imp_evs[-1:] if imp_evs else []), inc=pe)
            vv = S_.step("vector", lambda e, i=i, qt=qt: e.scalar_tensor_tensor(
                out=imp_sb[:, i * 256:(i + 1) * 256], in0=pA[:, 0:256],
                scalar=SCALE / 4.0, in1=cmadd_sb[:, qt * 256:(qt + 1) * 256],
                op0=ALU.mult, op1=ALU.add), waits=[(pe, pv)], inc=ev)
            imp_evs.append((ev, vv))

        nqt = len(qts)
        imp3 = imp_sb[:, 0:nqt * 256].rearrange("p (t c) -> p t c", c=256)
        scr3 = scr[:, 0:nqt * 256].rearrange("p (t c) -> p t c", c=256)
        S_.step("vector", lambda e: e.reduce_max(out=selst[:, 0:nqt], in_=imp3, axis=AX.X),
                waits=[imp_evs[-1]])
        for h0 in range(0, nqt, 8):
            hn = min(8, nqt - h0)
            S_.step("vector", lambda e, h0=h0, hn=hn: e.match_replace(
                out=scr3[:, h0:h0 + hn, :], in_to_replace=selst[:, h0:h0 + hn],
                in_values=imp3[:, h0:h0 + hn, :], imm_value=NEGBIG))
        for it in range(1, 4):
            S_.step("vector", lambda e, it=it: e.reduce_max(out=selst[:, it * 16:it * 16 + nqt],
                                                            in_=scr3, axis=AX.X))
            if it < 3:
                for h0 in range(0, nqt, 8):
                    hn = min(8, nqt - h0)
                    S_.step("vector", lambda e, it=it, h0=h0, hn=hn: e.match_replace(
                        out=scr3[:, h0:h0 + hn, :], in_to_replace=selst[:, it * 16 + h0:it * 16 + h0 + hn],
                        in_values=scr3[:, h0:h0 + hn, :], imm_value=NEGBIG))
        S_.step("vector", lambda e: e.tensor_scalar_max(selst[:, 64:64 + nqt], selst[:, 0:nqt], -1000.0))
        mpv = S_.step("vector", lambda e: e.tensor_scalar_mul(selst[:, 80:80 + nqt],
                                                              selst[:, 64:64 + nqt], -1.0), inc=ev)
        for i, qt in enumerate(qts):
            S_.step("scalar", lambda e, i=i, qt=qt: e.activation(
                out=scr2[:, 0:256], in_=imp_sb[:, i * 256:(i + 1) * 256], func=AF.Exp,
                bias=selst[:, 80 + i:81 + i], accum_out=selst[:, 96 + i:97 + i]),
                waits=[(ev, mpv), p.const_ready])
        S_.step("scalar", lambda e: e.activation(out=selst[:, 112:112 + nqt],
                                                 in_=selst[:, 64:64 + nqt], func=AF.Exp,
                                                 scale=-1.0, bias=-1000.0))
        sgs = S_.last_sig["scalar"]
        S_.step("vector", lambda e: e.tensor_add(selst[:, 96:96 + nqt], selst[:, 96:96 + nqt],
                                                 selst[:, 112:112 + nqt]), waits=[sgs])
        sg = S_.last_sig["vector"]
        S_.step("scalar", lambda e: e.activation(out=selst[:, 112:112 + nqt],
                                                 in_=selst[:, 96:96 + nqt], func=AF.Ln),
                waits=[sg])
        sg = S_.last_sig["scalar"]
        S_.step("vector", lambda e: e.tensor_add(selst[:, 112:112 + nqt], selst[:, 112:112 + nqt],
                                                 selst[:, 64:64 + nqt]), waits=[sg])
        S_.step("vector", lambda e: e.tensor_scalar_add(selst[:, 112:112 + nqt],
                                                        selst[:, 112:112 + nqt], LN1E10))
        S_.step("vector", lambda e: e.tensor_tensor(out=selst[:, 112:112 + nqt],
                                                    in0=selst[:, 112:112 + nqt],
                                                    in1=selst[:, 48:48 + nqt], op=ALU.max))
        mbv = S_.step("vector", lambda e: e.tensor_tensor(
            out=mb_sb[:, 0:nqt * 256].rearrange("p (t c) -> p t c", c=256),
            in0=imp3,
            in1=selst[:, 112:112 + nqt].unsqueeze(2).broadcast_to([128, nqt, 256]),
            op=ALU.is_ge), inc=ev)
        sel_done = (ev, mbv)

        prev_qt_sig = ()
        for i, qt in enumerate(qts):
            kts_fine = list(range(qt + 1)) + list(range(16, 16 + qt + 1))

            tw = [sel_done, tabs] + list(prev_qt_sig)
            t1 = S_.step("tensor", lambda e, i=i: e.transpose(
                pT[:, 0:128], mb_sb[:, i * 256:i * 256 + 128], id_sb[:]), waits=tw, inc=pe)
            mv1 = S_.step("vector", lambda e: e.tensor_copy(out=mbt_sb[:, 0:128], in_=pT[:, 0:128]),
                          waits=[(pe, t1)], inc=ev)
            t2 = S_.step("tensor", lambda e, i=i: e.transpose(
                pT[:, 128:256], mb_sb[:, i * 256 + 128:i * 256 + 256], id_sb[:]),
                waits=[(ev, mv1)], inc=pe)
            mv = S_.step("vector", lambda e: e.tensor_copy(out=mbt_sb[:, 128:256], in_=pT[:, 128:256]),
                         waits=[(pe, t2)], inc=ev)
            mbt_done = (ev, mv)

            qrhs = qt_sb[:].rearrange("p (g t) -> p g t", g=G)[:, :, qt * 128:(qt + 1) * 128]
            c1 = S_.step("tensor", lambda e, qrhs=qrhs: e.matmul(
                pA[:, 512:1024], ckbf[:, 0:128], qrhs, start=True, stop=True),
                waits=[lq, lc] + list(prev_qt_sig), inc=pe)
            c2 = S_.step("tensor", lambda e, qrhs=qrhs: e.matmul(
                pA[:, 1024:1536], ckbf[:, 128:256], qrhs, start=True, stop=True), inc=pe)
            c3 = S_.step("tensor", lambda e, qrhs=qrhs, kvh=kvh: e.matmul(
                pA[0:1, 0:512], memkt_sb[:, kvh:kvh + 1], qrhs, start=True, stop=True), inc=pe)
            s1 = S_.step("scalar", lambda e: e.activation(out=pcA_sb[:], in_=pA[:, 512:1024],
                                                          func=AF.Exp, scale=SCALE, bias=-SHIFT),
                         waits=[(pe, c1), p.const_ready], inc=evs)
            s2 = S_.step("scalar", lambda e: e.activation(out=pcB_sb[:], in_=pA[:, 1024:1536],
                                                          func=AF.Exp, scale=SCALE, bias=-SHIFT),
                         waits=[(pe, c2)], inc=evs)
            s3 = S_.step("scalar", lambda e: e.activation(out=pcm_sb[:], in_=pA[0:1, 0:512],
                                                          func=AF.Exp, scale=SCALE, bias=-SHIFT),
                         waits=[(pe, c3)], inc=evs)
            m1 = S_.step("vector", lambda e, qt=qt: e.tensor_tensor(
                out=pcA_sb[:].rearrange("p (g t) -> p g t", g=G),
                in0=pcA_sb[:].rearrange("p (g t) -> p g t", g=G),
                in1=cm01_sb[:, qt * 128:(qt + 1) * 128].unsqueeze(1).broadcast_to([128, G, 128]),
                op=ALU.mult), waits=[(evs, s1), tabs], inc=ev)
            m2 = S_.step("vector", lambda e, qt=qt: e.tensor_tensor(
                out=pcB_sb[:].rearrange("p (g t) -> p g t", g=G),
                in0=pcB_sb[:].rearrange("p (g t) -> p g t", g=G),
                in1=cm01_sb[:, TL + qt * 128:TL + (qt + 1) * 128].unsqueeze(1).broadcast_to([128, G, 128]),
                op=ALU.mult), waits=[(evs, s2)], inc=ev)
            def cpv(e, kvh=kvh):
                ins = None
                for g in range(G):
                    ins = e.matmul(pB[:, g * 128:(g + 1) * 128], pcA_sb[:, g * 128:(g + 1) * 128],
                                   cvA[:], start=(g == 0), stop=False)
                    ins = e.matmul(pB[:, g * 128:(g + 1) * 128], pcB_sb[:, g * 128:(g + 1) * 128],
                                   cvB[:], start=False, stop=False)
                    ins = e.matmul(pB[:, g * 128:(g + 1) * 128], pcm_sb[0:1, g * 128:(g + 1) * 128],
                                   memv_sb[0:1, 0:128], start=False, stop=(g == G - 1))
                for g in range(G):
                    ins = e.matmul(pB[:, 512 + g:513 + g], pcA_sb[:, g * 128:(g + 1) * 128],
                                   ones_sb[:, 0:1], start=(g == 0), stop=False)
                    ins = e.matmul(pB[:, 512 + g:513 + g], pcB_sb[:, g * 128:(g + 1) * 128],
                                   ones_sb[:, 0:1], start=False, stop=False)
                    ins = e.matmul(pB[:, 512 + g:513 + g], pcm_sb[0:1, g * 128:(g + 1) * 128],
                                   ones_sb[0:1, 0:1], start=False, stop=(g == G - 1))
                return ins
            cp = S_.step("tensor", cpv, waits=[(ev, m1), (ev, m2), (evs, s3), ones_ready], inc=pe)
            ce = S_.step("scalar", lambda e: e.copy(out=c_un[:], in_=pB[:, 0:512]),
                         waits=[(pe, cp)], inc=evs)
            S_.step("scalar", lambda e: e.activation(out=rz_sb[:, 12:16], in_=pB[:, 512:516], func=AF.Ln))
            zc = S_.step("scalar", lambda e: e.activation(out=rz_sb[:, 0:4], in_=rz_sb[:, 12:16],
                                                          func=AF.Exp, scale=-1.0), inc=evs)
            comp_done = [(evs, ce), (evs, zc)]

            slid_map = {qt: 0, 16 + qt: 2}
            if qt >= 1:
                slid_map[16 + qt - 1] = 1
            slid_kts = sorted(slid_map.keys())
            prev_exp = {}
            prev_m16 = None
            prev_pv = None
            for ki, kt in enumerate(kts_fine if qt not in skip_fine else []):
                bi = ki % 2
                sb_off = 0 if bi == 0 else 512
                sw = [lq, lv] + comp_done + (list(prev_exp.get(bi, ())))
                sm = S_.step("tensor", lambda e, kt=kt, sb_off=sb_off, qt=qt: e.matmul(
                    pB[:, sb_off:sb_off + 512], rk_sb[:, kt * 128:(kt + 1) * 128],
                    rqt_sb[:].rearrange("p (g t) -> p g t", g=G)[:, :, qt * 128:(qt + 1) * 128],
                    start=True, stop=True), waits=sw, inc=pe)
                mw = [mbt_done, tabs] + ([prev_m16] if prev_m16 else [])
                m16 = S_.step("tensor", lambda e, kt=kt: e.matmul(
                    pA[:, 0:128], ew_sb[:, (kt % 16) * 128:((kt % 16) + 1) * 128],
                    mbt_sb[:, 0:128] if kt < 16 else mbt_sb[:, 128:256],
                    start=True, stop=True), waits=mw, inc=pe)
                se = S_.step("scalar", lambda e, bi=bi, sb_off=sb_off: e.activation(
                    out=pf_sb[bi][:], in_=pB[:, sb_off:sb_off + 512],
                    func=AF.Exp, scale=SCALE, bias=-SHIFT), waits=[(pe, sm), p.const_ready], inc=evs)
                if kt == qt:
                    vm = S_.step("vector", lambda e: e.tensor_tensor(
                        out=m16b_sb[:], in0=pA[:, 0:128], in1=own_sb[:], op=ALU.max),
                        waits=[(pe, m16), tabs], inc=ev)
                else:
                    vm = S_.step("vector", lambda e: e.tensor_copy(out=m16b_sb[:], in_=pA[:, 0:128]),
                                 waits=[(pe, m16)], inc=ev)
                prev_m16 = (ev, vm)
                pm = S_.step("vector", lambda e, bi=bi: e.tensor_tensor(
                    out=pfm_sb[bi][:].rearrange("p (g t) -> p g t", g=G),
                    in0=pf_sb[bi][:].rearrange("p (g t) -> p g t", g=G),
                    in1=m16b_sb[:].unsqueeze(1).broadcast_to([128, G, 128]),
                    op=ALU.mult), waits=[(evs, se), (ev, vm)], inc=ev)
                def fpv(e, kt=kt, bi=bi, first=(ki == 0), last=(ki == len(kts_fine) - 1)):
                    ins = None
                    for g in range(G):
                        ins = e.matmul(pB[:, 1024 + g * 128:1024 + (g + 1) * 128],
                                       pfm_sb[bi][:, g * 128:(g + 1) * 128],
                                       v_sb[:, kt * 144:kt * 144 + 128],
                                       start=(first and g == 0), stop=(last and g == G - 1))
                        ins = e.matmul(pB[:, 1536 + g:1537 + g],
                                       pfm_sb[bi][:, g * 128:(g + 1) * 128],
                                       ones_sb[:, 0:1], start=(first and g == 0), stop=(last and g == G - 1))
                    return ins
                pvv = S_.step("tensor", fpv, waits=[(ev, pm)], inc=pe)
                prev_exp[bi] = ((pe, pvv),)
                if kt in slid_kts and not no_slid:
                    slot = slid_map[kt]
                    sfirst = (kt == slid_kts[0])
                    slast = (kt == slid_kts[-1])
                    ps = S_.step("vector", lambda e, bi=bi, slot=slot: e.tensor_tensor(
                        out=psm_sb[:].rearrange("p (g t) -> p g t", g=G),
                        in0=pf_sb[bi][:].rearrange("p (g t) -> p g t", g=G),
                        in1=slid_sb[:, slot * 128:(slot + 1) * 128].unsqueeze(1).broadcast_to([128, G, 128]),
                        op=ALU.mult), waits=[(evs, se), tabs] + ([prev_pv] if prev_pv else []), inc=ev)
                    def spv(e, kt=kt, sfirst=sfirst, slast=slast):
                        ins = None
                        for g in range(G):
                            ins = e.matmul(pA[:, 512 + g * 128:512 + (g + 1) * 128],
                                           psm_sb[:, g * 128:(g + 1) * 128],
                                           v_sb[:, kt * 144:kt * 144 + 128],
                                           start=(sfirst and g == 0), stop=(slast and g == G - 1))
                            ins = e.matmul(pA[:, 1024 + g:1025 + g],
                                           psm_sb[:, g * 128:(g + 1) * 128],
                                           ones_sb[:, 0:1], start=(sfirst and g == 0), stop=(slast and g == G - 1))
                        return ins
                    spvv = S_.step("tensor", spv, waits=[(ev, ps)], inc=pe)
                    prev_pv = (pe, spvv)

            if qt in skip_fine:
                zf = S_.step("scalar", lambda e: e.activation(out=rz_sb[:, 4:8], in_=rz_sb[:, 4:8],
                                                              func=AF.Identity, scale=0.0, bias=1.0), inc=evs)
            else:
                last_fine = prev_exp[(len(kts_fine) - 1) % 2][0]
                S_.step("scalar", lambda e: e.activation(out=rz_sb[:, 12:16], in_=pB[:, 1536:1540], func=AF.Ln),
                        waits=[last_fine])
                zf = S_.step("scalar", lambda e: e.activation(out=rz_sb[:, 4:8], in_=rz_sb[:, 12:16],
                                                              func=AF.Exp, scale=-1.0), inc=evs)
            if no_slid:
                zs = S_.step("scalar", lambda e: e.activation(out=rz_sb[:, 8:12], in_=rz_sb[:, 8:12], func=AF.Identity, scale=0.0, bias=1.0), inc=evs)
            else:
                S_.step("scalar", lambda e: e.activation(out=rz_sb[:, 12:16], in_=pA[:, 1024:1028], func=AF.Ln),
                        waits=([prev_pv] if prev_pv else []))
                zs = S_.step("scalar", lambda e: e.activation(out=rz_sb[:, 8:12], in_=rz_sb[:, 12:16],
                                                              func=AF.Exp, scale=-1.0), inc=evs)
            goff = qt * 96 + 12 * kvh
            for br in range(3):
                S_.step("vector", lambda e, br=br, goff=goff: e.tensor_tensor(
                    out=e_sb[:, br * 4:(br + 1) * 4],
                    in0=gates_sb[:, goff + br:goff + br + 12].rearrange("p (g three) -> p g three", three=3)[:, :, 0],
                    in1=rz_sb[:, br * 4:(br + 1) * 4], op=ALU.mult),
                    waits=[(evs, zf), (evs, zs), tabs])
            eg = S_.last_sig["vector"]
            if debug:
                for g_ in range(G):
                    S_.step("vector", lambda e, g_=g_: e.tensor_scalar(
                        out=dbg_sb[:, g_ * 128:(g_ + 1) * 128], in0=c_un[:, g_ * 128:(g_ + 1) * 128],
                        scalar1=rz_sb[:, g_:g_ + 1], scalar2=None, op0=ALU.mult),
                        waits=[eg] + list(ao_tok.get("dbg", ())))
                    S_.step("vector", lambda e, g_=g_: e.tensor_scalar(
                        out=dbg_sb[:, 512 + g_ * 128:512 + (g_ + 1) * 128],
                        in0=pB[:, 1024 + g_ * 128:1024 + (g_ + 1) * 128],
                        scalar1=rz_sb[:, 4 + g_:5 + g_], scalar2=None, op0=ALU.mult))
                    S_.step("vector", lambda e, g_=g_: e.tensor_scalar(
                        out=dbg_sb[:, 1024 + g_ * 128:1024 + (g_ + 1) * 128],
                        in0=pA[:, 512 + g_ * 128:512 + (g_ + 1) * 128],
                        scalar1=rz_sb[:, 8 + g_:9 + g_], scalar2=None, op0=ALU.mult))
                S_.step("vector", lambda e: e.tensor_copy(out=dbgz_sb[:, 0:16], in_=rz_sb[:]))
                S_.step("vector", lambda e: e.tensor_copy(out=dbgz_sb[:, 16:20], in_=pB[:, 1536:1540]))
                S_.step("vector", lambda e: e.tensor_copy(out=dbgz_sb[:, 20:24], in_=pA[:, 1024:1028]))
                dsig = S_.last_sig["vector"]
                S_.dma("sync", DBGC[qt * 128:(qt + 1) * 128, kvh * 512:(kvh + 1) * 512],
                       dbg_sb[:, 0:512], "dbg", waits=[dsig])
                S_.dma("sync", DBGF[qt * 128:(qt + 1) * 128, kvh * 512:(kvh + 1) * 512],
                       dbg_sb[:, 512:1024], "dbg", waits=[dsig])
                S_.dma("sync", DBGS[qt * 128:(qt + 1) * 128, kvh * 512:(kvh + 1) * 512],
                       dbg_sb[:, 1024:1536], "dbg", waits=[dsig])
                dtk = S_.dma("sync", DBGZ[qt * 128:(qt + 1) * 128, :], dbgz_sb[:], "dbg", waits=[dsig])
                ao_tok["dbg"] = (dtk,)
                dbg_last = dtk
                out_tokens.append(dtk)
            bo = i % 2
            for g in range(G):
                S_.step("vector", lambda e, g=g: e.tensor_scalar(
                    out=acc_sb[:], in0=c_un[:, g * 128:(g + 1) * 128],
                    scalar1=e_sb[:, g:g + 1], scalar2=None, op0=ALU.mult),
                    waits=[eg] + ([ao_tok[bo]] if bo in ao_tok else []))
                S_.step("vector", lambda e, g=g: e.scalar_tensor_tensor(
                    out=acc_sb[:], in0=pB[:, 1024 + g * 128:1024 + (g + 1) * 128],
                    scalar=e_sb[:, 4 + g:5 + g], in1=acc_sb[:],
                    op0=ALU.mult, op1=ALU.add))
                S_.step("vector", lambda e, g=g, bo=bo: e.scalar_tensor_tensor(
                    out=aout_sb[bo][:, g * 128:(g + 1) * 128],
                    in0=pA[:, 512 + g * 128:512 + (g + 1) * 128],
                    scalar=e_sb[:, 8 + g:9 + g], in1=acc_sb[:],
                    op0=ALU.mult, op1=ALU.add))
            mg = S_.last_sig["vector"]
            da = S_.dma("sync", AOUT[qt * 128:(qt + 1) * 128, kvh * 512:(kvh + 1) * 512],
                        aout_sb[bo][:], f"ao{bo}", waits=[mg])
            ao_tok[bo] = da
            out_tokens.append(da)
            prev_qt_sig = (mg, da) + ((dbg_last,) if debug else ())
        kvh_loads_prev = (S_.last_sig["vector"], (pe, S_.val(pe)))

    lasts = {}
    for tok in out_tokens:
        lasts[id(tok[0])] = tok
    p.finish(list(lasts.values()))
    return p


def build_matmul_T(name, Kd, Nd):
    p = Prog(name)
    nc, S_ = p.nc, p.sched
    X = p.inp("X", [TL, Kd], BF16)
    W2 = p.inp("W2", [128, (Nd // 128) * (Kd // 128) * 128], F32)
    YT = p.out("YT", [Nd, TL], BF16)
    p.add_consts()
    nK = Kd // 128
    nN = Nd // 128
    xlt = p.sb("xlt", [128, nK * TL], BF16)
    wti = [p.sb(f"wt{i}", [128, nK * 128], BF16) for i in range(2)]
    ysb = [p.sb(f"ysb{i}", [128, TL], BF16) for i in range(2)]
    pA = p.ps("pA", [128, TL])
    pe = p.sem("pe"); ev = p.sem("ev")
    xl_all = S_.dma_group("sync",
                          [(xlt[:, f * TL:(f + 1) * TL], X[:, f * 128:(f + 1) * 128])
                           for f in range(nK)], "xlt", transpose=True)
    prev_ev = None
    st_tok = {}
    for o in range(nN):
        wtok = S_.dma("gpsimd", wti[o % 2][:], W2[:, o * nK * 128:(o + 1) * nK * 128],
                      f"w{o % 2}", waits=[st_tok[o - 2]] if o >= 2 else ())
        def mmfn(e, o=o):
            ins = None
            for f in range(nK):
                for ch in range(4):
                    ins = e.matmul(pA[:, ch * 512:(ch + 1) * 512],
                                   wti[o % 2][:, f * 128:(f + 1) * 128],
                                   xlt[:, f * TL + ch * 512: f * TL + (ch + 1) * 512],
                                   start=(f == 0), stop=(f == nK - 1))
            return ins
        mw = [wtok, xl_all] + ([prev_ev] if prev_ev else [])
        pv = S_.step("tensor", mmfn, waits=mw, inc=pe)
        evw = [(pe, pv)] + ([st_tok[o - 2]] if o >= 2 else [])
        evv = S_.step("scalar", lambda e, o=o: e.copy(out=ysb[o % 2][:], in_=pA[:]),
                      waits=evw, inc=ev)
        prev_ev = (ev, evv)
        st_tok[o] = S_.dma("sync", YT[o * 128:(o + 1) * 128, :], ysb[o % 2][:],
                           f"y{o % 2}", waits=[(ev, evv)])
    p.finish([st_tok[nN - 1], st_tok[nN - 2]])
    return p


def build_l0ffn(name="l0ffn", CH=256):
    p = Prog(name)
    nc, S_ = p.nc, p.sched
    XLN = p.inp("XLN", [TL, D], BF16)
    U1W = p.inp("U1W", [128, 8 * 32 * 128], F32)
    V1W = p.inp("V1W", [128, 172 * 8 * 128], F32)
    U2W = p.inp("U2W", [128, 8 * 86 * 128], F32)
    V2W = p.inp("V2W", [128, 32 * 8 * 128], F32)
    B1G = p.inp("B1G", [128, 86], F32)
    B1U = p.inp("B1U", [128, 86], F32)
    B2 = p.inp("B2", [128, 32], F32)
    YT = p.out("YT", [D, TL], BF16)
    p.add_consts()
    NCH = TL // CH

    xlt = p.sb("xlt", [128, 32 * TL], BF16)
    ptsb = p.sb("ptsb", [128, 8 * TL], BF16)
    u1sb = [p.sb(f"u1sb{i}", [128, 32 * 128], BF16) for i in range(2)]
    v1g = [p.sb(f"v1g{i}", [128, 8 * 128], BF16) for i in range(2)]
    v1u = [p.sb(f"v1u{i}", [128, 8 * 128], BF16) for i in range(2)]
    u2s = [p.sb(f"u2s{i}", [128, 8 * 128], BF16) for i in range(2)]
    v2s = [p.sb(f"v2s{i}", [128, 8 * 128], BF16) for i in range(2)]
    b1g = p.sb("b1g", [128, 86], F32)
    b1u = p.sb("b1u", [128, 86], F32)
    b2sb = p.sb("b2sb", [128, 32], F32)
    silu_sb = p.sb("silu_sb", [128, CH], BF16)
    act_sb = [p.sb(f"act_sb{i}", [128, CH], BF16) for i in range(2)]
    ztsb = p.sb("ztsb", [128, 8 * CH], BF16)
    ysb = [p.sb(f"ysb{i}", [128, CH], BF16) for i in range(2)]
    pA = p.ps("pA", [128, 1024])
    pB = p.ps("pB", [128, 8 * 256])
    pC = p.ps("pC", [128, 512])
    pe = p.sem("pe"); ev = p.sem("ev")
    assert CH == 256

    xl_all = S_.dma_group("sync",
                          [(xlt[:, f * TL:(f + 1) * TL], XLN[:, f * 128:(f + 1) * 128])
                           for f in range(32)], "xlt", transpose=True)
    bt = S_.dma_group("gpsimd", [(b1g[:], B1G), (b1u[:], B1U), (b2sb[:], B2)], "bt")

    pt_ev = None
    u1_tok = {}
    for r in range(8):
        u1_tok[r] = S_.dma("gpsimd", u1sb[r % 2][:], U1W[:, r * 32 * 128:(r + 1) * 32 * 128],
                           f"u1{r % 2}", waits=[pt_ev] if r >= 2 and pt_ev else ())
        def ptfn(e, r=r):
            ins = None
            for f in range(32):
                for ch in range(4):
                    ins = e.matmul(pB[:, (ch % 4) * 512:(ch % 4) * 512 + 512],
                                   u1sb[r % 2][:, f * 128:(f + 1) * 128],
                                   xlt[:, f * TL + ch * 512: f * TL + (ch + 1) * 512],
                                   start=(f == 0), stop=(f == 31))
            return ins
        pv = S_.step("tensor", ptfn, waits=[u1_tok[r], xl_all] + ([pt_ev] if pt_ev else []), inc=pe)
        evv = S_.step("scalar", lambda e, r=r: e.copy(out=ptsb[:, r * TL:(r + 1) * TL], in_=pB[:, 0:2048]),
                      waits=[(pe, pv)], inc=ev)
        pt_ev = (ev, evv)

    y_tok = {}
    y_prev = {}
    prev_zev = None
    for ci in range(NCH):
        t0 = ci * CH
        act_tok = {}
        last_z = None
        h_done = {}
        z_done = {}
        for it in range(86):
            bi = it % 2
            lw1 = [h_done[bi]] if bi in h_done else []
            lw2 = [z_done[bi]] if bi in z_done else []
            wv = [S_.dma("gpsimd", v1g[bi][:], V1W[:, it * 1024:(it + 1) * 1024], f"vg{bi}", waits=lw1),
                  S_.dma("gpsimd", v1u[bi][:], V1W[:, (86 + it) * 1024:(87 + it) * 1024], f"vu{bi}"),
                  S_.dma("gpsimd", u2s[bi][:].rearrange("p (mo m) -> p mo m", mo=8),
                         U2W.rearrange("p (mo kf m) -> p mo kf m", mo=8, kf=86)[:, :, it, :],
                         f"u2{bi}", waits=lw2)]
            def hfn(e, it=it, bi=bi, t0=t0):
                ins = None
                for rr in range(8):
                    ins = e.matmul(pA[:, 0:CH], v1g[bi][:, rr * 128:(rr + 1) * 128],
                                   ptsb[:, rr * TL + t0: rr * TL + t0 + CH],
                                   start=(rr == 0), stop=(rr == 7))
                for rr in range(8):
                    ins = e.matmul(pA[:, 512:512 + CH], v1u[bi][:, rr * 128:(rr + 1) * 128],
                                   ptsb[:, rr * TL + t0: rr * TL + t0 + CH],
                                   start=(rr == 0), stop=(rr == 7))
                return ins
            hw = wv + [(ev, S_.val(ev))] + ([pt_ev] if pt_ev else [])
            pt_ev = None
            pv = S_.step("tensor", hfn, waits=hw, inc=pe)
            h_done[bi] = (pe, pv)
            sv = S_.step("scalar", lambda e, it=it: e.activation(
                out=silu_sb[:], in_=pA[:, 0:CH], func=AF.Silu,
                bias=b1g[:, it:it + 1]), waits=[(pe, pv), bt], inc=ev)
            mv = S_.step("vector", lambda e, it=it, bi=bi: e.scalar_tensor_tensor(
                out=act_sb[bi][:], in0=pA[:, 512:512 + CH],
                scalar=b1u[:, it:it + 1], in1=silu_sb[:],
                op0=ALU.add, op1=ALU.mult), waits=[(ev, sv)] + list(act_tok.get(bi, ())), inc=ev)
            def zfn(e, it=it, bi=bi):
                ins = None
                for rr in range(8):
                    ins = e.matmul(pB[:, rr * 256:rr * 256 + CH],
                                   u2s[bi][:, rr * 128:(rr + 1) * 128], act_sb[bi][:],
                                   start=(it == 0 and rr % 2 == 0), stop=(it == 85 and rr % 2 == 1))
                return ins
            zw = [(ev, mv)] + ([prev_zev] if (it == 0 and prev_zev) else [])
            zv = S_.step("tensor", zfn, waits=zw, inc=pe)
            act_tok[bi] = ((pe, zv),)
            z_done[bi] = (pe, zv)
            last_z = zv
        zev = S_.step("scalar", lambda e: e.copy(out=ztsb[:], in_=pB[:, 0:2048]),
                      waits=[(pe, last_z)], inc=ev)
        prev_zev = (ev, zev)
        prev_yv = None
        for fo in range(32):
            bo = fo % 2
            wv2 = S_.dma("gpsimd", v2s[bo][:], V2W[:, fo * 1024:(fo + 1) * 1024], f"v2{bo}",
                         waits=[y_prev[bo]] if bo in y_prev else ())
            def yfn(e, fo=fo, bo=bo):
                ins = None
                for rr in range(8):
                    ins = e.matmul(pC[:, 0:CH], v2s[bo][:, rr * 128:(rr + 1) * 128],
                                   ztsb[:, rr * CH:(rr + 1) * CH],
                                   start=(rr == 0), stop=(rr == 7))
                return ins
            yw = [wv2, (ev, zev)] + ([prev_yv] if prev_yv else [])
            pv = S_.step("tensor", yfn, waits=yw, inc=pe)
            yv = S_.step("scalar", lambda e, fo=fo, bo=bo: e.activation(
                out=ysb[bo][:], in_=pC[:, 0:CH], func=AF.Identity,
                bias=b2sb[:, fo:fo + 1]),
                waits=[(pe, pv)] + ([y_prev[bo]] if bo in y_prev else []), inc=ev)
            prev_yv = (ev, yv)
            y_tok[(ci, fo)] = S_.dma("sync", YT[fo * 128:(fo + 1) * 128, t0:t0 + CH],
                                     ysb[bo][:], f"yst{bo}", waits=[(ev, yv)])
            y_prev[bo] = y_tok[(ci, fo)]
    p.finish([y_tok[(NCH - 1, 31)], y_tok[(NCH - 1, 30)]])
    return p


def build_l1attn(name="l1attn", qt_list=None):
    qts = qt_list if qt_list is not None else list(range(NT))
    p = Prog(name)
    nc, S_ = p.nc, p.sched
    Q1T = p.inp("Q1T", [D, TL], BF16)
    KV0 = p.inp("KV0", [256, TL], BF16)
    KV1 = p.inp("KV1", [256, TL], BF16)
    ALIB = p.inp("ALIB", [NT * 32 * 128, H], F32)
    ADDM = p.inp("ADDM", [3 * H * 128, 128], F32)
    DTAB = p.inp("DTAB", [128, H], F32)
    AOUT = p.out("AOUT", [TL, D], BF16)
    p.add_consts()

    q1sb = p.sb("q1sb", [128, H * TL], BF16)
    k1sb = p.sb("k1sb", [128, 2 * TL], BF16)
    v_sb = p.sb("v_sb", [128, 32 * 144], BF16)
    alib_sb = [p.sb(f"alib{i}", [128, 32 * H], BF16) for i in range(2)]
    addm_sb = [p.sb(f"addm_sb{i}", [128, 3 * 8 * 128], F32) for i in range(2)]
    dt_sb = p.sb("dt_sb", [128, H], F32)
    pf_sb = p.sb("pf1", [128, 1024], BF16)
    pfm_sb = p.sb("pfm1", [128, 1024], BF16)
    om_sb = p.sb("om_sb", [128, 1536], F32)
    rz_sb = p.sb("rz1_sb", [128, 24], F32)
    ao_sb = [p.sb(f"ao1_{i}", [128, 1024], BF16) for i in range(2)]
    pS = p.ps("pS", [128, 1024])
    pOm = p.ps("pOm", [128, 1536])
    pOa = p.ps("pOa", [128, 1536])
    pe = p.sem("pe"); ev = p.sem("ev"); evs = p.sem("evs")

    def o_off(h):
        return (h // 3) * 512 + (h % 3) * 129

    vset = S_.step("vector", lambda e: e.memset(v_sb[:], 1.0))
    ones_ready = S_.last_sig["vector"]
    dtk = S_.dma("gpsimd", dt_sb[:], DTAB, "dt")
    lq = S_.dma_group("sync",
                      [(q1sb[:, h * TL:(h + 1) * TL], Q1T[h * 128:(h + 1) * 128, :])
                       for h in range(H)] +
                      [(k1sb[:, 0:TL], KV0[0:128, :]), (k1sb[:, TL:2 * TL], KV1[0:128, :])],
                      "lq")
    lv = S_.dma_group("sync",
                      [(v_sb[:, kt * 144:kt * 144 + 128],
                        (KV0 if kt < 16 else KV1)[128:256, (kt % 16) * 128:((kt % 16) + 1) * 128])
                       for kt in range(32)],
                      "lv", waits=[ones_ready], transpose=True)

    out_tokens = []
    ao_tok = {}
    prev_qt = ()
    al_tok = {}
    am_rd = {}
    for i, qt in enumerate(qts):
        bi = i % 2
        kts = list(range(qt + 1)) + list(range(16, 16 + qt + 1))
        slid_map = {qt: 0, 16 + qt: 2}
        if qt >= 1:
            slid_map[16 + qt - 1] = 1
        mult_kts = [kt for kt in kts if kt not in slid_map]
        add_kts = [kt for kt in kts if kt in slid_map]
        atk = S_.dma_group("gpsimd",
                           [(alib_sb[bi][:, kt * H:(kt + 1) * H],
                             ALIB[(qt * 32 + kt) * 128:(qt * 32 + kt + 1) * 128, :])
                            for kt in range(32)],
                           f"al{bi}", waits=list(al_tok.get(bi, ())))
        qt_sigs = []
        for hg in range(4):
            hb = hg % 2
            amk = S_.dma_group("gpsimd",
                               [(addm_sb[hb][:, (sl_ * 8 + hh) * 128:(sl_ * 8 + hh + 1) * 128],
                                 ADDM[(sl_ * H + hg * 8 + hh) * 128:(sl_ * H + hg * 8 + hh + 1) * 128, :])
                                for sl_ in range(3) for hh in range(8)],
                               f"am{hb}", waits=list(am_rd.get(hb, ())))
            prev_pv = ()
            for ki, kt in enumerate(kts):
                sw = [lq] + list(prev_pv) + list(prev_qt)
                def sfn(e, kt=kt, hg=hg, qt=qt):
                    ins = None
                    for half in range(2):
                        rhs = q1sb[:].rearrange("p (h t) -> p h t", h=H)[
                            :, hg * 8 + half * 4: hg * 8 + (half + 1) * 4,
                            qt * 128:(qt + 1) * 128]
                        ins = e.matmul(pS[:, half * 512:(half + 1) * 512],
                                       k1sb[:, kt * 128:(kt + 1) * 128], rhs,
                                       start=True, stop=True)
                    return ins
                sv_ = S_.step("tensor", sfn, waits=sw, inc=pe)
                if kt in slid_map:
                    slot = slid_map[kt]
                    for hh in range(8):
                        S_.step("vector", lambda e, slot=slot, hh=hh, hb=hb: e.scalar_tensor_tensor(
                            out=pS[:, hh * 128:(hh + 1) * 128],
                            in0=addm_sb[hb][:, (slot * 8 + hh) * 128:(slot * 8 + hh + 1) * 128],
                            scalar=1.0 / SCALE, in1=pS[:, hh * 128:(hh + 1) * 128],
                            op0=ALU.mult, op1=ALU.add),
                            waits=[(pe, sv_), amk] if hh == 0 else [amk])
                    am = S_.last_sig["vector"]
                    se = S_.step("scalar", lambda e: e.activation(
                        out=pfm_sb[:], in_=pS[:], func=AF.Exp, scale=SCALE, bias=-SHIFT),
                        waits=[am, p.const_ready], inc=evs)
                    dst = pOa
                    first = (kt == add_kts[0])
                    last = (kt == add_kts[-1])
                    pmtok = (evs, se)
                else:
                    se = S_.step("scalar", lambda e: e.activation(
                        out=pf_sb[:], in_=pS[:], func=AF.Exp, scale=SCALE, bias=-SHIFT),
                        waits=[(pe, sv_), p.const_ready], inc=evs)
                    pm = S_.step("vector", lambda e, kt=kt, hg=hg, bi=bi: e.tensor_tensor(
                        out=pfm_sb[:].rearrange("p (h t) -> p h t", h=8),
                        in0=pf_sb[:].rearrange("p (h t) -> p h t", h=8),
                        in1=alib_sb[bi][:, kt * H + hg * 8: kt * H + (hg + 1) * 8]
                            .unsqueeze(2).broadcast_to([128, 8, 128]),
                        op=ALU.mult), waits=[(evs, se), atk], inc=ev)
                    dst = pOm
                    first = (kt == mult_kts[0])
                    last = (kt == mult_kts[-1])
                    pmtok = (ev, pm)
                def pvfn(e, kt=kt, dst=dst, first=first, last=last):
                    ins = None
                    started = set()
                    for h in range(8):
                        bank = o_off(h) // 512
                        st = first and (bank not in started)
                        started.add(bank)
                        ins = e.matmul(dst[:, o_off(h):o_off(h) + 129],
                                       pfm_sb[:, h * 128:(h + 1) * 128],
                                       v_sb[:, kt * 144:kt * 144 + 129],
                                       start=st, stop=(last and (h in (2, 5, 7))))
                    return ins
                pvv = S_.step("tensor", pvfn, waits=[pmtok, lv], inc=pe)
                prev_pv = ((pe, pvv),)
            if mult_kts:
                S_.step("scalar", lambda e: e.copy(out=om_sb[:, 0:387], in_=pOm[:, 0:387]),
                        waits=[prev_pv[0]])
                S_.step("scalar", lambda e: e.copy(out=om_sb[:, 512:899], in_=pOm[:, 512:899]))
                ce = S_.step("scalar", lambda e: e.copy(out=om_sb[:, 1024:1282], in_=pOm[:, 1024:1282]),
                             inc=evs)
                for h in range(8):
                    S_.step("vector", lambda e, h=h, hg=hg: e.scalar_tensor_tensor(
                        out=om_sb[:, o_off(h):o_off(h) + 129],
                        in0=om_sb[:, o_off(h):o_off(h) + 129],
                        scalar=dt_sb[:, hg * 8 + h: hg * 8 + h + 1],
                        in1=pOa[:, o_off(h):o_off(h) + 129],
                        op0=ALU.mult, op1=ALU.add),
                        waits=[(evs, ce), dtk] if h == 0 else [])
                msig = S_.last_sig["vector"]
            else:
                S_.step("scalar", lambda e: e.copy(out=om_sb[:, 0:387], in_=pOa[:, 0:387]),
                        waits=[prev_pv[0]])
                S_.step("scalar", lambda e: e.copy(out=om_sb[:, 512:899], in_=pOa[:, 512:899]))
                ce = S_.step("scalar", lambda e: e.copy(out=om_sb[:, 1024:1282], in_=pOa[:, 1024:1282]),
                             inc=evs)
                msig = (evs, ce)
            for bk in range(3):
                nh = 3 if bk < 2 else 2
                S_.step("scalar", lambda e, bk=bk, nh=nh: e.activation(
                    out=rz_sb[:, 12 + bk * 3:12 + bk * 3 + nh],
                    in_=om_sb[:, bk * 512 + 128: bk * 512 + 128 + nh * 129]
                        .rearrange("p (a b) -> p a b", b=129)[:, :, 0],
                    func=AF.Ln), waits=[msig] if bk == 0 else [])
            zz = S_.step("scalar", lambda e: e.activation(
                out=rz_sb[:, 0:8], in_=rz_sb[:, 12:20], func=AF.Exp, scale=-1.0), inc=evs)
            for h in range(8):
                S_.step("scalar", lambda e, h=h, bi=bi: e.activation(
                    out=ao_sb[bi][:, h * 128:(h + 1) * 128],
                    in_=om_sb[:, o_off(h):o_off(h) + 128],
                    func=AF.Identity, scale=rz_sb[:, h:h + 1]),
                    waits=([(evs, zz)] + list(ao_tok.get(bi, ()))) if h == 0 else [])
            asig = S_.last_sig["scalar"]
            da = S_.dma("sync", AOUT[qt * 128:(qt + 1) * 128, hg * 1024:(hg + 1) * 1024],
                        ao_sb[bi][:], f"ao{bi}", waits=[asig])
            ao_tok[bi] = (da,)
            am_rd[hb] = (S_.last_sig["vector"], asig)
            out_tokens.append(da)
            qt_sigs = [asig, da]
        al_tok[bi] = (S_.last_sig["vector"],)
        prev_qt = tuple(qt_sigs)
    lasts = {}
    for tok in out_tokens:
        lasts[id(tok[0])] = tok
    p.finish(list(lasts.values()))
    return p


def build_l1ffn(name="l1ffn"):
    p = Prog(name)
    nc, S_ = p.nc, p.sched
    XLN = p.inp("XLN", [TL, D], BF16)
    AW = p.inp("AW", [128, 32 * 128], F32)
    BW = p.inp("BW", [128, 32 * 128], F32)
    WRT = p.inp("WRT", [128, 32 * 8], F32)
    BRT = p.inp("BRT", [128, 8], F32)
    E16 = p.inp("E16", [8, 128], F32)
    IDENT = p.inp("IDENT", [128, 128], F32)
    YT = p.out("YT", [D, TL], BF16)
    p.add_consts()

    xlt = p.sb("xlt", [128, 32 * TL], BF16)
    awsb = p.sb("awsb", [128, 32 * 128], BF16)
    bwsb = p.sb("bwsb", [128, 32 * 128], BF16)
    wrsb = p.sb("wrsb", [128, 32 * 8], BF16)
    brsb = p.sb("brsb", [128, 8], F32)
    e16sb = p.sb("e16sb", [8, 128], BF16)
    idsb = p.sb("idsb", [128, 128], F32)
    xasb = p.sb("xasb", [128, TL], BF16)
    rst = p.sb("rst", [128, 16], F32)
    sc_sb = p.sb("sc_sb", [128, 16], F32)
    scT_sb = p.sb("scT_sb", [8, TL], BF16)
    zasb = p.sb("zasb", [128, TL], BF16)
    ysb = [p.sb(f"ysb{i}", [128, TL], BF16) for i in range(2)]
    pA = p.ps("pA", [128, TL])
    pB = p.ps("pB", [128, TL])
    pe = p.sem("pe"); ev = p.sem("ev"); evs = p.sem("evs")

    xl_all = S_.dma_group("sync",
                          [(xlt[:, f * TL:(f + 1) * TL], XLN[:, f * 128:(f + 1) * 128])
                           for f in range(32)], "xlt", transpose=True)
    wt = S_.dma_group("gpsimd", [(awsb[:], AW), (bwsb[:], BW), (wrsb[:], WRT),
                                 (brsb[:], BRT), (e16sb[:], E16), (idsb[:], IDENT)], "wt")

    def xafn(e):
        ins = None
        for f in range(32):
            for ch in range(4):
                ins = e.matmul(pA[:, ch * 512:(ch + 1) * 512], awsb[:, f * 128:(f + 1) * 128],
                               xlt[:, f * TL + ch * 512: f * TL + (ch + 1) * 512],
                               start=(f == 0), stop=(f == 31))
        return ins
    pv = S_.step("tensor", xafn, waits=[wt, xl_all], inc=pe)
    xv = S_.step("scalar", lambda e: e.copy(out=xasb[:], in_=pA[:]), waits=[(pe, pv)], inc=evs)
    xa_done = (evs, xv)

    prev_tr = None
    for t in range(NT):
        def rfn(e, t=t):
            ins = None
            for f in range(32):
                ins = e.matmul(pB[:, 0:8], xlt[:, f * TL + t * 128: f * TL + (t + 1) * 128],
                               wrsb[:, f * 8:(f + 1) * 8], start=(f == 0), stop=(f == 31))
            return ins
        w = [wt, xl_all] + ([prev_tr] if prev_tr else [(pe, S_.val(pe))])
        pv = S_.step("tensor", rfn, waits=w, inc=pe)
        S_.step("vector", lambda e: e.tensor_tensor(out=sc_sb[:, 0:8], in0=pB[:, 0:8],
                                                    in1=brsb[:], op=ALU.add), waits=[(pe, pv)])
        S_.step("vector", lambda e: e.reduce_max(out=rst[:, 0:1], in_=sc_sb[:, 0:8], axis=AX.X))
        mv = S_.step("vector", lambda e: e.tensor_scalar_mul(rst[:, 1:2], rst[:, 0:1], -1.0), inc=ev)
        s1 = S_.step("scalar", lambda e: e.activation(out=sc_sb[:, 0:8], in_=sc_sb[:, 0:8],
                                                      func=AF.Exp, bias=rst[:, 1:2],
                                                      accum_out=rst[:, 2:3]),
                     waits=[(ev, mv), p.const_ready], inc=evs)
        S_.step("scalar", lambda e: e.activation(out=rst[:, 3:4], in_=rst[:, 2:3], func=AF.Ln))
        s2 = S_.step("scalar", lambda e: e.activation(out=rst[:, 4:5], in_=rst[:, 3:4],
                                                      func=AF.Exp, scale=-1.0), inc=evs)
        vv = S_.step("vector", lambda e: e.tensor_scalar(out=sc_sb[:, 8:16], in0=sc_sb[:, 0:8],
                                                         scalar1=rst[:, 4:5], scalar2=8.0,
                                                         op0=ALU.mult, op1=ALU.mult),
                     waits=[(evs, s2)], inc=ev)
        tv = S_.step("tensor", lambda e: e.transpose(pB[0:8, 512:640], sc_sb[:, 8:16], idsb[:]),
                     waits=[(ev, vv), wt] + ([prev_tr] if prev_tr else []), inc=pe)
        cv = S_.step("vector", lambda e, t=t: e.tensor_copy(out=scT_sb[:, t * 128:(t + 1) * 128],
                                                            in_=pB[0:8, 512:640]),
                     waits=[(pe, tv)], inc=ev)
        prev_tr = (ev, cv)

    def sexfn(e):
        ins = None
        for ch in range(4):
            ins = e.matmul(pB[:, ch * 512:(ch + 1) * 512], e16sb[:],
                           scT_sb[:, ch * 512:(ch + 1) * 512], start=True, stop=True)
        return ins
    pv = S_.step("tensor", sexfn, waits=[prev_tr, wt], inc=pe)
    zv = S_.step("vector", lambda e: e.tensor_tensor(out=zasb[:], in0=xasb[:], in1=pB[:],
                                                     op=ALU.mult), waits=[(pe, pv), xa_done], inc=ev)
    za_done = (ev, zv)

    st_tok = {}
    prev_y = None
    for fo in range(32):
        bo = fo % 2
        def yfn(e, fo=fo):
            ins = None
            for ch in range(4):
                ins = e.matmul(pA[:, ch * 512:(ch + 1) * 512], bwsb[:, fo * 128:(fo + 1) * 128],
                               zasb[:, ch * 512:(ch + 1) * 512], start=True, stop=True)
            return ins
        w = [za_done] + ([prev_y] if prev_y else [(evs, S_.val(evs))])
        pv = S_.step("tensor", yfn, waits=w, inc=pe)
        yv = S_.step("scalar", lambda e, fo=fo, bo=bo: e.activation(
            out=ysb[bo][:], in_=pA[:], func=AF.Identity, scale=2.0),
            waits=[(pe, pv)] + ([st_tok[fo - 2]] if fo >= 2 else []), inc=evs)
        prev_y = (evs, yv)
        st_tok[fo] = S_.dma("sync", YT[fo * 128:(fo + 1) * 128, :], ysb[bo][:], f"y{bo}",
                            waits=[(evs, yv)])
    p.finish([st_tok[31], st_tok[30]])
    return p


def shard_x(x):
    out = []
    for c in range(8):
        b, par = c // 2, c % 2
        out.append(np.ascontiguousarray(
            x[b].reshape(32, 128, D)[par::2].reshape(TL, D)))
    return out

def unshard_x(shards):
    x = np.zeros((B, S, D), np.float32)
    for c in range(8):
        b, par = c // 2, c % 2
        x[b].reshape(32, 128, D)[par::2] = np.asarray(shards[c], np.float32).reshape(NT, 128, D)
    return x

_PROGS = {}

def _prog(key, builder):
    if key not in _PROGS:
        _PROGS[key] = builder()
    return _PROGS[key]

def _run(p, in_maps):
    in_maps = [{k: v for k, v in m.items() if k in p.ins} for m in in_maps]
    return run_bass_kernel_spmd(p.nc, in_maps, core_ids=list(range(NCORE))).results

def kernel(**inputs):
    cn = consts_np()
    x = np.ascontiguousarray(inputs["L0_attn_hidden_states"], dtype=np.float32)
    xs = shard_x(x)
    tabs = [host_tables(par) for par in (0, 1)]
    alibi = [host_alibi(par) for par in (0, 1)]

    sh, per = prep_l0proj_inputs(inputs)
    w_o2 = host_lhsT(np.ascontiguousarray(inputs["L0_attn_w_o"].T)).reshape(128, -1)
    ffn = {
        "U1W": host_lhsT(np.ascontiguousarray(inputs["L0_ffn_U1"])).reshape(128, -1),
        "V1W": host_lhsT(np.ascontiguousarray(inputs["L0_ffn_V1"])).reshape(128, -1),
        "U2W": host_lhsT(np.ascontiguousarray(inputs["L0_ffn_U2"])).reshape(128, -1),
        "V2W": host_lhsT(np.ascontiguousarray(inputs["L0_ffn_V2"])).reshape(128, -1),
        "B1G": np.ascontiguousarray(inputs["L0_ffn_b1"][:INTER].reshape(86, 128).T),
        "B1U": np.ascontiguousarray(inputs["L0_ffn_b1"][INTER:].reshape(86, 128).T),
        "B2": np.ascontiguousarray(inputs["L0_ffn_b2"].reshape(32, 128).T),
        "consts_": cn,
    }
    wq2 = host_lhsT(np.ascontiguousarray(inputs["L1_attn_Wq"])).reshape(128, -1)
    wkv2 = host_lhsT(np.ascontiguousarray(inputs["L1_attn_Wkv"])).reshape(128, -1)
    wo12 = host_lhsT(np.ascontiguousarray(inputs["L1_attn_Wo"])).reshape(128, -1)
    Acat = np.ascontiguousarray(inputs["L1_ffn_A"].transpose(2, 0, 1).reshape(D, 128))
    Bcat = np.ascontiguousarray(inputs["L1_ffn_B"].transpose(0, 2, 1).reshape(128, D))
    wrT = np.ascontiguousarray(inputs["L1_ffn_Wr"].T)
    E16 = np.zeros((8, 128), np.float32)
    for h in range(8):
        E16[h, h * 16:(h + 1) * 16] = 1
    l1f = {
        "AW": host_lhsT(Acat).reshape(128, -1),
        "BW": host_lhsT(Bcat).reshape(128, -1),
        "WRT": np.ascontiguousarray(wrT.reshape(32, 128, 8).transpose(1, 0, 2)).reshape(128, -1),
        "BRT": np.tile(inputs["L1_ffn_br"][None, :], (128, 1)).astype(np.float32),
        "E16": E16, "IDENT": np.eye(128, dtype=np.float32), "consts_": cn,
    }

    p = _prog("ln0", lambda: build_ln("ln0", with_add=False))
    r_ln0 = _run(p, [{"X": xs[c], "consts_": cn} for c in range(8)])

    p = _prog("l0proj", build_l0proj)
    ims = []
    for c in range(8):
        m = dict(sh); m.update(per[c % 2]); m["consts_"] = cn
        m["XLN"] = np.asarray(r_ln0[c]["XLN"])
        ims.append(m)
    r_proj = _run(p, ims)

    p = _prog("l0comp", build_l0compress)
    ims = []
    for c in range(8):
        m = dict(sh); m["consts_"] = cn
        m["KPK"] = np.asarray(r_proj[c]["KPK"]); m["VPK"] = np.asarray(r_proj[c]["VPK"])
        ims.append(m)
    r_comp = _run(p, ims)

    p = _prog("l0attn", build_l0attn)
    ims = []
    for c in range(8):
        b, par = c // 2, c % 2
        cp = c + 1 - 2 * par
        t = tabs[par]
        m = {"QT": np.asarray(r_proj[c]["QT"]), "RQT": np.asarray(r_proj[c]["RQT"]),
             "QBART": np.asarray(r_proj[c]["QBART"]),
             "RKT0": np.asarray(r_proj[c]["RKT"]), "RKT1": np.asarray(r_proj[cp]["RKT"]),
             "VT0": np.asarray(r_proj[c]["VT"]), "VT1": np.asarray(r_proj[cp]["VT"]),
             "CKT0": np.asarray(r_comp[c]["CKT"]), "CKT1": np.asarray(r_comp[cp]["CKT"]),
             "CV0": np.asarray(r_comp[c]["CV"]), "CV1": np.asarray(r_comp[cp]["CV"]),
             "MEMKT": np.ascontiguousarray(inputs["L0_attn_mem_kv"][0, :, 0, :].T).astype(np.float32),
             "MEMV": np.ascontiguousarray(inputs["L0_attn_mem_kv"][1, :, 0, :]).astype(np.float32),
             "GATES": np.asarray(r_proj[c]["GATES"]),
             "CMADD": t["CMADD"], "CM01T": t["CM01T"], "EWIDE": t["EWIDE"],
             "OWND": t["OWND"], "SLID": t["SLID"].reshape(384, 128), "IDENT": t["IDENT"],
             "consts_": cn}
        ims.append(m)
    r_attn = _run(p, ims)

    p = _prog("wo0", lambda: build_matmul_T("wo0", D, D))
    r_wo = _run(p, [{"X": np.asarray(r_attn[c]["AOUT"]), "W2": w_o2, "consts_": cn}
                    for c in range(8)])

    p = _prog("ln1", lambda: build_ln("ln1", with_add=True))
    r_ln1 = _run(p, [{"X": xs[c], "YT": np.asarray(r_wo[c]["YT"]), "consts_": cn}
                     for c in range(8)])

    p = _prog("l0ffn", build_l0ffn)
    ims = []
    for c in range(8):
        m = dict(ffn)
        m["XLN"] = np.asarray(r_ln1[c]["XLN"])
        ims.append(m)
    r_ffn = _run(p, ims)

    p = _prog("ln2", lambda: build_ln("ln2", with_add=True))
    r_ln2 = _run(p, [{"X": np.asarray(r_ln1[c]["XOUT"]), "YT": np.asarray(r_ffn[c]["YT"]),
                      "consts_": cn} for c in range(8)])

    p = _prog("q1", lambda: build_matmul_T("q1", D, D))
    r_q1 = _run(p, [{"X": np.asarray(r_ln2[c]["XLN"]), "W2": wq2, "consts_": cn}
                    for c in range(8)])
    p = _prog("kv1", lambda: build_matmul_T("kv1", D, 256))
    r_kv1 = _run(p, [{"X": np.asarray(r_ln2[c]["XLN"]), "W2": wkv2, "consts_": cn}
                     for c in range(8)])

    p = _prog("l1attn", build_l1attn)
    ims = []
    for c in range(8):
        b, par = c // 2, c % 2
        cp = c + 1 - 2 * par
        A, ADDM, DTAB = alibi[par]
        m = {"Q1T": np.asarray(r_q1[c]["YT"]),
             "KV0": np.asarray(r_kv1[c]["YT"]), "KV1": np.asarray(r_kv1[cp]["YT"]),
             "ALIB": A.reshape(-1, H), "ADDM": ADDM.reshape(-1, 128), "DTAB": DTAB,
             "consts_": cn}
        ims.append(m)
    r_a1 = _run(p, ims)

    p = _prog("wo1", lambda: build_matmul_T("wo1", D, D))
    r_wo1 = _run(p, [{"X": np.asarray(r_a1[c]["AOUT"]), "W2": wo12, "consts_": cn}
                     for c in range(8)])

    p = _prog("ln3", lambda: build_ln("ln3", with_add=True))
    r_ln3 = _run(p, [{"X": np.asarray(r_ln2[c]["XOUT"]), "YT": np.asarray(r_wo1[c]["YT"]),
                      "consts_": cn} for c in range(8)])

    p = _prog("l1ffn", build_l1ffn)
    ims = []
    for c in range(8):
        m = dict(l1f)
        m["XLN"] = np.asarray(r_ln3[c]["XLN"])
        ims.append(m)
    r_f1 = _run(p, ims)

    p = _prog("fin", lambda: build_ln("fin", with_add=True, final_out_f32=True))
    r_fin = _run(p, [{"X": np.asarray(r_ln3[c]["XOUT"]), "YT": np.asarray(r_f1[c]["YT"]),
                      "consts_": cn} for c in range(8)])

    return unshard_x([r_fin[c]["XOUT"] for c in range(8)])

def prep_l0proj_inputs(inputs):
    w_qkv = inputs["L0_attn_w_qkv"]
    sh = {}
    sh["W2"] = host_lhsT(np.ascontiguousarray(w_qkv.T)).reshape(128, -1)
    wg = np.ascontiguousarray(inputs["L0_attn_w_gate"].T)
    sh["WG2"] = np.ascontiguousarray(wg.reshape(32, 128, 96).transpose(1, 0, 2)).reshape(128, -1)
    sh["BG"] = np.tile(inputs["L0_attn_b_gate"][None, :], (128, 1)).astype(np.float32)
    sh["KW1T2"] = host_lhsT(np.ascontiguousarray(inputs["L0_attn_k_w1"].T)).reshape(128, -1)
    sh["VW1T2"] = host_lhsT(np.ascontiguousarray(inputs["L0_attn_v_w1"].T)).reshape(128, -1)
    sh["KW2T2"] = host_lhsT(np.ascontiguousarray(inputs["L0_attn_k_w2"].T)).reshape(128, -1)
    vw2t = np.ascontiguousarray(inputs["L0_attn_v_w2"].T)
    sh["VW2R"] = np.ascontiguousarray(vw2t.reshape(16, 128, 128).transpose(1, 0, 2)).reshape(128, -1)
    sh["KB1"] = np.ascontiguousarray(inputs["L0_attn_k_b1"].reshape(16, 128).T)
    sh["VB1"] = np.ascontiguousarray(inputs["L0_attn_v_b1"].reshape(16, 128).T)
    sh["KB2"] = inputs["L0_attn_k_b2"].reshape(128, 1).astype(np.float32)
    sh["VB2T"] = np.tile(inputs["L0_attn_v_b2"][None, :], (128, 1)).astype(np.float32)
    sh["KPT"] = np.ascontiguousarray(inputs["L0_attn_k_pos"].transpose(2, 0, 1)).reshape(128, -1)
    sh["VPT"] = np.ascontiguousarray(inputs["L0_attn_v_pos"].transpose(2, 0, 1)).reshape(128, -1)
    sh["consts_"] = consts_np()
    per = []
    for par in (0, 1):
        t = host_tables(par)
        per.append({"COST": t["COST"], "SINT": t["SINT"], "RT": t["RT"]})
    return sh, per
